# version 02
# baseline (speedup 1.0000x reference)
import numpy as np
import ml_dtypes
from contextlib import ExitStack

import concourse.bass as bass
import concourse.mybir as mybir
from concourse.bass_utils import run_bass_kernel_spmd

F32 = mybir.dt.float32
BF16 = mybir.dt.bfloat16
I32 = mybir.dt.int32
AF = mybir.ActivationFunctionType
ALU = mybir.AluOpType
AX = mybir.AxisListType
bf16 = ml_dtypes.bfloat16

D = 2048
S = 2048
NC = 16
TS = 512
NSLAB = S // TS
NH = 16
HD = 128
EPS = 1e-5
NEG = -1e30

ENGS = ("sync", "tensor", "vector", "scalar", "gpsimd")


class Sem:
    __slots__ = ("h", "n", "name")

    def __init__(self, handle, name):
        self.h = handle
        self.n = 0
        self.name = name

    def bump(self, amt=1):
        self.n += amt
        return self.n


class KB:

    def __init__(self, nc, ctx):
        self.nc = nc
        self.ctx = ctx
        self.stream = {e: [] for e in ENGS}
        self._semid = 0

    def sem(self, name=None):
        self._semid += 1
        name = name or f"s{self._semid}"
        return Sem(self.ctx.enter_context(self.nc.semaphore(name)), name)

    def sb(self, name, shape, dtype):
        return self.ctx.enter_context(self.nc.sbuf_tensor(name, shape, dtype))

    def ps(self, name, shape, dtype):
        return self.ctx.enter_context(self.nc.psum_tensor(name, shape, dtype))

    def op(self, eng, build, waits=(), inc=None, inc_amt=1):
        wl = tuple((s.h, v) for (s, v) in waits)
        ih = inc.h if inc is not None else None
        use_drain = eng in ("vector", "scalar", "gpsimd")

        def f(e, build=build, wl=wl, ih=ih, amt=inc_amt, dr=use_drain):
            for h, v in wl:
                e.wait_ge(h, v)
            ins = build(e)
            if ih is not None:
                if dr:
                    e.drain().then_inc(ih, amt)
                else:
                    ins.then_inc(ih, amt)

        self.stream[eng].append(f)
        return inc.bump(inc_amt) if inc is not None else None

    def wait(self, eng, waits):
        wl = tuple((s.h, v) for (s, v) in waits)

        def f(e, wl=wl):
            for h, v in wl:
                e.wait_ge(h, v)

        self.stream[eng].append(f)

    def dma(self, eng, out, in_, waits=(), inc=None, transpose=False):
        wl = tuple((s.h, v) for (s, v) in waits)
        ih = inc.h if inc is not None else None

        def f(e, o=out, i=in_, wl=wl, ih=ih, tr=transpose):
            for h, v in wl:
                e.wait_ge(h, v)
            ins = (e.dma_start_transpose(o, i) if tr
                   else e.dma_start(out=o, in_=i))
            if ih is not None:
                ins.then_inc(ih, 16)

        self.stream[eng].append(f)
        return inc.bump(16) if inc is not None else None

    def fence_sync(self, cm):
        self.wait("sync", [(cm.sv, cm.sv.n), (cm.ss, cm.ss.n),
                           (cm.sp, cm.sp.n)])

    def finalize(self):
        block = self.ctx.enter_context(self.nc.Block())
        for e in ENGS:
            fns = self.stream[e]
            getattr(block, e)(lambda eng, fns=fns: [f(eng) for f in fns])


def host_consts():
    c = {}
    c["identity"] = np.eye(128, dtype=bf16)
    R = np.zeros((128, 128), dtype=np.float32)
    for m in range(64):
        R[m, m + 64] = -1.0
    for m in range(64, 128):
        R[m, m - 64] = 1.0
    c["rot"] = np.ascontiguousarray(R.T).astype(bf16)

    i = np.arange(128)[:, None]
    cc = np.arange(1024)[None, :]
    c["mask"] = np.where(cc <= i + 512, 0.0, NEG).astype(bf16)

    half = HD // 2
    inv_freq = 1.0 / (10000.0 ** (np.arange(half, dtype=np.float64) / half))
    pos = np.arange(S, dtype=np.float64)
    freqs = np.outer(pos, inv_freq)
    freqs = np.concatenate([freqs, freqs], axis=-1)
    cos = np.cos(freqs)
    sin = np.sin(freqs)
    scale = (pos + 512 / 2.0) / 512
    xs = np.stack([np.broadcast_to(scale[:, None], (S, half)),
                   1.0 / np.broadcast_to(scale[:, None], (S, half))],
                  axis=-1).reshape(S, HD)
    sc = 1.0 / np.sqrt(HD)
    c["l0_cq"] = np.ascontiguousarray((cos * xs * sc).T).astype(bf16)
    c["l0_sq"] = np.ascontiguousarray((sin * xs * sc).T).astype(bf16)
    c["l0_ck"] = np.ascontiguousarray((cos / xs).T).astype(bf16)
    c["l0_sk"] = np.ascontiguousarray((sin / xs).T).astype(bf16)

    inv2 = 1.0 / (10000.0 ** (np.arange(0, HD, 2, dtype=np.float64) / HD))
    f2 = np.outer(pos, inv2)
    c2 = np.sin(f2)
    s2 = np.cos(f2)
    C2 = np.concatenate([c2, c2], axis=-1)
    S2_ = np.concatenate([s2, s2], axis=-1)
    c["l2_cq"] = np.ascontiguousarray((C2 * sc).T).astype(bf16)
    c["l2_sq"] = np.ascontiguousarray((S2_ * sc).T).astype(bf16)
    c["l2_ck"] = np.ascontiguousarray(C2.T).astype(bf16)
    c["l2_sk"] = np.ascontiguousarray(S2_.T).astype(bf16)
    return c


_HOST_CONSTS = None


def get_host_consts():
    global _HOST_CONSTS
    if _HOST_CONSTS is None:
        _HOST_CONSTS = host_consts()
    return _HOST_CONSTS


class Common:

    def __init__(self, kb: KB, need=()):
        nc = kb.nc
        self.kb = kb
        need = set(need)
        self.identity = nc.dram_tensor("c_identity", [128, 128], BF16,
                                       kind="ExternalInput").ap()
        self.c_id = kb.sb("t_identity", [128, 128], BF16)
        if "attn" in need:
            self.rot = nc.dram_tensor("c_rot", [128, 128], BF16,
                                      kind="ExternalInput").ap()
            self.mask = nc.dram_tensor("c_mask", [128, 1024], BF16,
                                       kind="ExternalInput").ap()
            self.t_rot = kb.sb("t_rot", [128, 128], BF16)
            self.t_mask = kb.sb("t_mask", [128, 1024], BF16)
        self.XS = kb.sb("XS", [128, 8192], F32)
        self.XB = kb.sb("XB", [128, 8192], BF16)
        self.XN = kb.sb("XN", [128, 8192], BF16)
        self.W1 = kb.sb("W1", [128, 8192], BF16)
        self.W2 = kb.sb("W2", [128, 8192], BF16)
        self.W3 = kb.sb("W3", [128, 8192], BF16)
        self.HG = kb.sb("HG", [128, 32768], BF16)
        self.P = kb.sb("Pbuf", [128, 2048], BF16)
        self.ab = self.P[:, 0:1024]
        self.ones_col = kb.sb("ones_col", [128, 1], BF16)
        self.ones_row = kb.sb("ones_row", [1, 128], BF16)
        self.negones_row = kb.sb("negones_row", [1, 128], BF16)
        self.eps_col = kb.sb("eps_col", [128, 1], F32)
        self.row_a = kb.sb("row_a", [1, 512], BF16)
        self.row_b = kb.sb("row_b", [1, 512], BF16)
        self.row_t0 = kb.sb("row_t0", [1, 512], F32)
        self.row_t1 = kb.sb("row_t1", [1, 512], F32)
        self.row_t2 = kb.sb("row_t2", [1, 512], F32)
        self.col_t0 = kb.sb("col_t0", [128, 1], F32)
        self.col_t1 = kb.sb("col_t1", [128, 1], F32)
        self.col_t2 = kb.sb("col_t2", [128, 1], F32)
        self.id32 = kb.sb("id32", [128, 128], F32)
        self.rowP = self.P[:].bitcast(F32)[0:1, :]
        self.SC = kb.ps("SC", [128, 2048], F32)
        self.PA = kb.ps("PA", [128, 512], F32)
        self.PB = kb.ps("PB", [128, 512], F32)
        self.PG = kb.ps("PG", [128, 512], F32)
        self.TP = kb.ps("TP", [128, 1024], BF16)
        self._w12_used = 0
        self._w3_used = 0
        self.sv = kb.sem("sv")
        self.ss = kb.sem("ss")
        self.sp = kb.sem("sp")
        self.sg = kb.sem("sg")
        self.sq_ld = kb.sem("sqld")
        self.sw = kb.sem("sw")
        self.swd = kb.sem("swd")
        self.skp = [kb.sem("skA"), kb.sem("skB")]
        self.svp = [kb.sem("svA"), kb.sem("svB")]
        self.sst = kb.sem("sst")
        self.spr = kb.sem("spr")

    def emit_init(self, extra_consts=()):
        kb = self.kb
        self.s_init = kb.sem("sinit")
        kb.dma("sync", self.c_id[:], self.identity, inc=self.s_init)
        if hasattr(self, "t_rot"):
            kb.dma("sync", self.t_rot[:], self.rot, inc=self.s_init)
            kb.dma("sync", self.t_mask[:], self.mask, inc=self.s_init)
        kb.op("vector", lambda v: v.memset(self.ones_col[:], 1.0), inc=self.sv)
        kb.op("vector", lambda v: v.memset(self.ones_row[:], 1.0), inc=self.sv)
        kb.op("vector", lambda v: v.memset(self.negones_row[:], -1.0), inc=self.sv)
        kb.op("vector", lambda v: v.memset(self.eps_col[:], EPS), inc=self.sv)
        kb.op("scalar", lambda s: s.copy(self.id32[:], self.c_id[:]),
              waits=[(self.s_init, self.s_init.n)], inc=self.ss)
        self.init_v = self.sv.n
        self.init_ld = self.sq_ld.n

    def emit_ln(self, xb_region=None, src=None, src_is_bf16=False,
                affine_cols=None, dst=None):
        kb = self.kb
        xs3 = (src if src is not None else
               self.XS[:].rearrange("p (c t) -> p c t", c=NC))
        xb = xb_region if xb_region is not None else self.W3
        xb3 = xb[:, 0:8192].rearrange("p (c t) -> p c t", c=NC)
        dst = dst if dst is not None else self.XN
        xn3 = dst[:].rearrange("p (c t) -> p c t", c=NC)
        sq = self.HG[:, 0:8192].rearrange("p (c t) -> p c t", c=NC)
        stats_sum = self.SC[0:1, 0:512]
        stats_sq = self.SC[0:1, 512:1024]
        a_bc = self.SC[:, 1024:1536]
        b_bc = self.SC[:, 1536:2048]

        if not src_is_bf16:
            kb.op("vector", lambda v: v.tensor_copy(xb[:, 0:8192], xs3.rearrange("p c t -> p (c t)")),
                  inc=self.sv)
            xv = xb3
        else:
            xv = xs3
        vcast = self.sv.n
        kb.op("scalar", lambda s: s.activation(
            self.HG[:, 0:8192], xv.rearrange("p c t -> p (c t)"), AF.Square),
            waits=[(self.sv, vcast)], inc=self.ss)
        ssq = self.ss.n
        for c in range(NC):
            kb.op("tensor", lambda t, c=c, xv=xv: t.matmul(
                stats_sum, self.ones_col[:], xv[:, c, :],
                start=(c == 0), stop=(c == NC - 1)),
                waits=([(self.sv, vcast)] if c == 0 else []))
        for c in range(NC):
            kb.op("tensor", lambda t, c=c: t.matmul(
                stats_sq, self.ones_col[:], sq[:, c, :],
                start=(c == 0), stop=(c == NC - 1)),
                waits=[(self.ss, ssq)] if c == 0 else [],
                inc=self.sp if c == NC - 1 else None)
        pstats = self.sp.n
        kb.op("vector", lambda v: v.tensor_scalar(
            self.row_t0[:], stats_sum, 1.0 / D, None, op0=ALU.mult),
            waits=[(self.sp, pstats)], inc=self.sv)
        kb.op("vector", lambda v: v.tensor_scalar(
            self.row_t1[:], stats_sq, 1.0 / D, None, op0=ALU.mult), inc=self.sv)
        kb.op("vector", lambda v: v.tensor_tensor(
            self.row_t2[:], self.row_t0[:], self.row_t0[:], op=ALU.mult),
            inc=self.sv)
        kb.op("vector", lambda v: v.tensor_tensor(
            self.row_t1[:], self.row_t1[:], self.row_t2[:], op=ALU.subtract),
            inc=self.sv)
        vvar = self.sv.n
        kb.op("scalar", lambda s: s.activation(
            self.row_t2[:], self.row_t1[:], AF.Sqrt, bias=self.eps_col[0:1, 0:1]),
            waits=[(self.sv, vvar)], inc=self.ss)
        srstd = self.ss.n
        kb.op("vector", lambda v: v.reciprocal(self.row_t1[:], self.row_t2[:]),
              waits=[(self.ss, srstd)], inc=self.sv)
        kb.op("vector", lambda v: v.tensor_tensor(
            self.row_t2[:], self.row_t0[:], self.row_t1[:], op=ALU.mult),
            inc=self.sv)
        kb.op("vector", lambda v: v.tensor_copy(self.row_a[:], self.row_t1[:]),
              inc=self.sv)
        kb.op("vector", lambda v: v.tensor_copy(self.row_b[:], self.row_t2[:]),
              inc=self.sv)
        vrows = self.sv.n
        kb.op("tensor", lambda t: t.matmul(
            a_bc, self.ones_row[:], self.row_a[:], start=True, stop=True),
            waits=[(self.sv, vrows)])
        kb.op("tensor", lambda t: t.matmul(
            b_bc, self.negones_row[:], self.row_b[:], start=True, stop=True),
            inc=self.sp)
        pbc = self.sp.n
        kb.op("scalar", lambda s: s.activation(
            self.ab[:, 0:512], a_bc, AF.Copy), waits=[(self.sp, pbc)])
        kb.op("scalar", lambda s: s.activation(
            self.ab[:, 512:1024], b_bc, AF.Copy), inc=self.ss)
        sab = self.ss.n
        abv = self.ab[:, 0:512].rearrange("p (o t) -> p o t", o=1).broadcast_to([128, NC, 512])
        bbv = self.ab[:, 512:1024].rearrange("p (o t) -> p o t", o=1).broadcast_to([128, NC, 512])
        kb.op("vector", lambda v: v.tensor_tensor(xn3, xv, abv, op=ALU.mult),
              waits=[(self.ss, sab)], inc=self.sv)
        if affine_cols is None:
            kb.op("vector", lambda v: v.tensor_tensor(xn3, xn3, bbv, op=ALU.add),
                  inc=self.sv)
        else:
            kb.op("vector", lambda v: v.tensor_tensor(xn3, xn3, bbv, op=ALU.add),
                  inc=self.sv)
            wcols, bcols = affine_cols
            for c in range(NC):
                kb.op("vector", lambda v, c=c: v.tensor_scalar(
                    xn3[:, c, :], xn3[:, c, :], wcols[:, c:c + 1],
                    bcols[:, c:c + 1], op0=ALU.mult, op1=ALU.add),
                    inc=self.sv if c == NC - 1 else None)
        return self.sv.n


def _view3(t, c=NC, w=512):
    return t[:].rearrange("p (c t) -> p c t", c=c)


def emit_ffn_glu(cm: Common, kb: KB, cfg):
    H = cfg["H"]
    HC = H // 128
    HGRP = 512
    NG = H // HGRP
    act = cfg["act"]
    outacc = cfg.get("outacc", False)
    t_gs = cfg.get("gate_s_cols")
    t_us = cfg.get("up_s_cols")
    t_ds = cfg.get("down_s_cols")

    gbuf = _view3(cm.W1)
    ubuf = _view3(cm.W2)
    hg3 = cm.HG[:].rearrange("p (h t) -> p h t", t=TS)
    if outacc:
        oacc3 = cm.HG[:, 0:16384].bitcast(F32).rearrange(
            "p (c t) -> p c t", c=NC)
        hgslot = lambda h: hg3[:, 32 + (h % 8), :]
    else:
        hgslot = lambda h: hg3[:, h, :]
    xs3 = _view3(cm.XS)
    xn3 = _view3(cm.XN)
    g_used = cm._w12_used
    d_used = cm._w3_used
    hg_done = {}
    acc_done = 0
    kb.fence_sync(cm)

    def emit_down_partial(dc, hc0, hc1, hg_index, first):
        pass

    for s in range(NSLAB):
        kb.dma("sync", xs3, cfg["x_in"][:, :, s * TS:(s + 1) * TS]
               .rearrange("c p t -> p c t"), inc=cm.sq_ld)
        xld = cm.sq_ld.n
        kb.wait("vector", [(cm.sq_ld, xld), (cm.sp, d_used)])
        vxn = cm.emit_ln()
        if outacc:
            kb.op("vector", lambda v: v.memset(
                cm.HG[:, 0:16384].bitcast(F32), 0.0), inc=cm.sv)
            acc_zero = cm.sv.n
        kb.wait("tensor", [(cm.sv, cm.sv.n)])

        psg = [cm.PA[:], cm.SC[:, 0:512]]
        psu = [cm.PB[:], cm.SC[:, 512:1024]]
        evac_v = 0
        for g in range(NG):
            kb.wait("sync", [(cm.sp, max(g_used, d_used))])
            kb.dma("sync", gbuf, cfg["gateT"][:, :, g * HGRP:(g + 1) * HGRP]
                   .rearrange("c p h -> p c h"), inc=cm.sw)
            kb.dma("sync", ubuf, cfg["upT"][:, :, g * HGRP:(g + 1) * HGRP]
                   .rearrange("c p h -> p c h"), inc=cm.sw)
            wld = cm.sw.n
            for hi_ in range(4):
                h = g * 4 + hi_
                pg = psg[h % 2]
                pu = psu[h % 2]
                w = [(cm.sw, wld)] if hi_ == 0 else []
                lookback = 2 if not outacc else 8
                if h >= lookback and (h - lookback) in hg_done:
                    w = w + [(cm.sv, hg_done[h - lookback])]
                for c in range(NC):
                    kb.op("tensor", lambda t, c=c, pg=pg, hi_=hi_: t.matmul(
                        pg, cm.W1[:, c * 512 + hi_ * 128: c * 512 + hi_ * 128 + 128],
                        xn3[:, c, :], start=(c == 0), stop=(c == NC - 1)),
                        waits=w if c == 0 else [])
                for c in range(NC):
                    kb.op("tensor", lambda t, c=c, pu=pu, hi_=hi_: t.matmul(
                        pu, cm.W2[:, c * 512 + hi_ * 128: c * 512 + hi_ * 128 + 128],
                        xn3[:, c, :], start=(c == 0), stop=(c == NC - 1)),
                        inc=cm.sp if c == NC - 1 else None)
                pmm = cm.sp.n
                hgt = hgslot(h)
                if act == "silu":
                    if t_gs is not None:
                        kb.op("scalar", lambda s_, pg=pg, h=h: s_.activation(
                            cm.P[:, 0:512], pg, AF.Silu, scale=t_gs[:, h:h + 1]),
                            waits=[(cm.sp, pmm)], inc=cm.ss)
                    else:
                        kb.op("scalar", lambda s_, pg=pg: s_.activation(
                            cm.P[:, 0:512], pg, AF.Silu),
                            waits=[(cm.sp, pmm)], inc=cm.ss)
                    sact = cm.ss.n
                    if t_us is not None:
                        kb.op("vector", lambda v, pu=pu, h=h, hgt=hgt: v.scalar_tensor_tensor(
                            hgt, pu, t_us[:, h:h + 1], cm.P[:, 0:512],
                            op0=ALU.mult, op1=ALU.mult),
                            waits=[(cm.ss, sact)], inc=cm.sv)
                    else:
                        kb.op("vector", lambda v, pu=pu, hgt=hgt: v.tensor_tensor(
                            hgt, pu, cm.P[:, 0:512], op=ALU.mult),
                            waits=[(cm.ss, sact)], inc=cm.sv)
                else:
                    kb.op("scalar", lambda s_, pg=pg: s_.activation(
                        cm.P[:, 0:512], pg, AF.Square),
                        waits=[(cm.sp, pmm)], inc=cm.ss)
                    s1 = cm.ss.n
                    kb.op("vector", lambda v: v.tensor_scalar(
                        cm.P[:, 512:1024], cm.P[:, 0:512], 0.044715, 1.0,
                        op0=ALU.mult, op1=ALU.add),
                        waits=[(cm.ss, s1)], inc=cm.sv)
                    kb.op("vector", lambda v, pg=pg: v.scalar_tensor_tensor(
                        cm.P[:, 512:1024], pg, 0.7978845608028654,
                        cm.P[:, 512:1024], op0=ALU.mult, op1=ALU.mult),
                        inc=cm.sv)
                    v2 = cm.sv.n
                    kb.op("scalar", lambda s_: s_.activation(
                        cm.P[:, 512:1024], cm.P[:, 512:1024], AF.Tanh),
                        waits=[(cm.sv, v2)], inc=cm.ss)
                    s2 = cm.ss.n
                    kb.op("vector", lambda v: v.tensor_scalar(
                        cm.P[:, 512:1024], cm.P[:, 512:1024], 0.5, 0.5,
                        op0=ALU.mult, op1=ALU.add),
                        waits=[(cm.ss, s2)], inc=cm.sv)
                    kb.op("vector", lambda v, pg=pg: v.tensor_tensor(
                        cm.P[:, 512:1024], cm.P[:, 512:1024], pg, op=ALU.mult),
                        inc=cm.sv)
                    kb.op("vector", lambda v, pu=pu, hgt=hgt: v.tensor_tensor(
                        hgt, cm.P[:, 512:1024], pu, op=ALU.mult), inc=cm.sv)
                hg_done[h] = cm.sv.n
            g_used = cm.sp.n
            if outacc and (g % 2 == 1):
                hc0 = (g - 1) * 4
                voct = cm.sv.n
                dbuf3 = cm.W3[:, 0:8 * 128].rearrange(
                    "p (hc d) -> p hc d", hc=8)
                for dc in range(NC):
                    w = [(cm.sp, d_used)]
                    if hc0 == 0 and dc == 0:
                        w.append((cm.sv, vxn))
                    kb.dma("sync", dbuf3,
                           cfg["downT"][dc][:, hc0:hc0 + 8, :],
                           waits=w, inc=cm.swd)
                    dld = cm.swd.n
                    pd = cm.PG[:]
                    tw = [(cm.swd, dld), (cm.sv, voct)]
                    if evac_v:
                        tw.append((cm.sv, evac_v))
                    kb.wait("tensor", tw)
                    for j in range(8):
                        kb.op("tensor", lambda t, j=j, pd=pd: t.matmul(
                            pd, cm.W3[:, j * 128:j * 128 + 128],
                            hgslot(hc0 + j), start=(j == 0), stop=(j == 7)),
                            inc=cm.sp if j == 7 else None)
                    pdone = cm.sp.n
                    d_used = pdone
                    w2 = [(cm.sp, pdone)]
                    if dc == 0 and hc0 == 0:
                        w2.append((cm.sv, acc_zero))
                    kb.op("vector", lambda v, dc=dc, pd=pd: v.tensor_tensor(
                        oacc3[:, dc, :], oacc3[:, dc, :], pd, op=ALU.add),
                        waits=w2, inc=cm.sv)
                    evac_v = cm.sv.n
                acc_done = cm.sv.n
                kb.wait("vector", [(cm.sp, d_used)])

        if outacc:
            for dc in range(NC):
                kb.op("vector", lambda v, dc=dc: v.scalar_tensor_tensor(
                    xs3[:, dc, :], oacc3[:, dc, :], 0.5, xs3[:, dc, :],
                    op0=ALU.mult, op1=ALU.add),
                    waits=[(cm.sv, acc_done)] if dc == 0 else [],
                    inc=cm.sv)
        else:
            vhg_all = cm.sv.n
            dbuf = cm.W3[:, 0:HC * 128].rearrange("p (hc d) -> p hc d", hc=HC)
            for dc in range(NC):
                w = [(cm.sp, d_used)]
                if dc == 0:
                    w.append((cm.sv, vxn))
                kb.dma("sync", dbuf, cfg["downT"][dc], waits=w, inc=cm.swd)
                dld = cm.swd.n
                pd = cm.PG[:]
                tw = [(cm.swd, dld), (cm.sv, vhg_all)]
                if evac_v:
                    tw.append((cm.sv, evac_v))
                kb.wait("tensor", tw)
                for hc in range(HC):
                    kb.op("tensor", lambda t, hc=hc, pd=pd: t.matmul(
                        pd, cm.W3[:, hc * 128:hc * 128 + 128], hg3[:, hc, :],
                        start=(hc == 0), stop=(hc == HC - 1)),
                        inc=cm.sp if hc == HC - 1 else None)
                pdone = cm.sp.n
                d_used = pdone
                if t_ds is not None:
                    kb.op("vector", lambda v, dc=dc, pd=pd: v.tensor_scalar(
                        cm.P[:, 0:512], pd, t_ds[:, dc:dc + 1], 0.5,
                        op0=ALU.mult, op1=ALU.mult),
                        waits=[(cm.sp, pdone)], inc=cm.sv)
                    kb.op("vector", lambda v, dc=dc: v.tensor_tensor(
                        xs3[:, dc, :], xs3[:, dc, :], cm.P[:, 0:512],
                        op=ALU.add), inc=cm.sv)
                else:
                    kb.op("vector", lambda v, dc=dc, pd=pd: v.scalar_tensor_tensor(
                        xs3[:, dc, :], pd, 0.5, xs3[:, dc, :],
                        op0=ALU.mult, op1=ALU.add),
                        waits=[(cm.sp, pdone)], inc=cm.sv)
                evac_v = cm.sv.n
        vdone = cm.sv.n
        kb.dma("sync", cfg["x_out"][:, :, s * TS:(s + 1) * TS]
               .rearrange("c p t -> p c t"), xs3,
               waits=[(cm.sv, vdone)], inc=cm.sst)
    cm._w12_used = g_used
    cm._w3_used = d_used
    kb.wait("sync", [(cm.sst, cm.sst.n)])


def emit_attn(cm: Common, kb: KB, cfg):
    nkv = cfg["nkv"]
    causal = cfg["causal"]
    rope = cfg.get("rope")
    bias = cfg.get("bias_cols")
    xs3 = _view3(cm.XS)
    xn3 = _view3(cm.XN)
    xb3 = _view3(cm.XB)
    w23 = _view3(cm.W2)

    st = dict(po_evac=[None, None], pg_free=0, tp_free=0, p_free=0,
              sc_free=0, kbuf_used=[0, 0], vbuf_used=[0, 0],
              ob_free=[0, 0, 0, 0])
    kb.fence_sync(cm)

    def proj(dst3, w_ap, nout, bias_col, qscale, rope_tabs, s, src3):
        pp = [cm.PA[:], cm.PB[:]]
        for ocg in range(nout // 4):
            ow = ocg * 4
            kb.wait("sync", [(cm.sp, cm._w12_used)])
            kb.dma("sync", cm.W1[:, 0:8192].rearrange(
                "p (c n) -> p c n", c=NC),
                w_ap[:, :, ow * 128:(ow + 4) * 128]
                .rearrange("c p n -> p c n"), inc=cm.sw)
            wld = cm.sw.n
            for oi in range(4):
                oc = ow + oi
                po = pp[oc % 2]
                w = [(cm.sw, wld)] if oi == 0 else []
                if st["po_evac"][oc % 2]:
                    w = w + [st["po_evac"][oc % 2]]
                for c in range(NC):
                    kb.op("tensor", lambda t, c=c, oi=oi, po=po: t.matmul(
                        po, cm.W1[:, c * 512 + oi * 128:
                                  c * 512 + oi * 128 + 128],
                        src3[:, c, :], start=(c == 0), stop=(c == NC - 1)),
                        waits=w if c == 0 else [],
                        inc=cm.sp if c == NC - 1 else None)
                pmm = cm.sp.n
                cm._w12_used = pmm
                if bias_col is not None and qscale is not None:
                    kb.op("vector", lambda v, oc=oc, po=po: v.tensor_scalar(
                        dst3[:, oc, :], po, bias_col[:, oc:oc + 1],
                        qscale[:, 0:1], op0=ALU.add, op1=ALU.mult),
                        waits=[(cm.sp, pmm)], inc=cm.sv)
                    st["po_evac"][oc % 2] = (cm.sv, cm.sv.n)
                elif bias_col is not None:
                    kb.op("vector", lambda v, oc=oc, po=po: v.tensor_scalar(
                        dst3[:, oc, :], po, bias_col[:, oc:oc + 1], None,
                        op0=ALU.add),
                        waits=[(cm.sp, pmm)], inc=cm.sv)
                    st["po_evac"][oc % 2] = (cm.sv, cm.sv.n)
                else:
                    kb.op("scalar", lambda s_, oc=oc, po=po: s_.activation(
                        dst3[:, oc, :], po, AF.Copy),
                        waits=[(cm.sp, pmm)], inc=cm.ss)
                    st["po_evac"][oc % 2] = (cm.ss, cm.ss.n)
                if rope_tabs is not None:
                    ctab, stab = rope_tabs
                    pr = cm.PG[:]
                    sevac = st["po_evac"][oc % 2]
                    w2 = [sevac]
                    if st["pg_free"]:
                        w2.append((cm.sv, st["pg_free"]))
                    kb.op("tensor", lambda t, oc=oc: t.matmul(
                        pr, cm.t_rot[:], dst3[:, oc, :], start=True, stop=True),
                        waits=w2, inc=cm.sp)
                    prot = cm.sp.n
                    kb.op("vector", lambda v, oc=oc, s=s: v.tensor_tensor(
                        cm.P[:, 0:512], dst3[:, oc, :],
                        ctab[:, s * TS:(s + 1) * TS], op=ALU.mult),
                        waits=[(cm.sp, prot)], inc=cm.sv)
                    kb.op("vector", lambda v, s=s, pr=pr: v.tensor_tensor(
                        cm.P[:, 512:1024], pr,
                        stab[:, s * TS:(s + 1) * TS], op=ALU.mult), inc=cm.sv)
                    kb.op("vector", lambda v, oc=oc: v.tensor_tensor(
                        dst3[:, oc, :], cm.P[:, 0:512], cm.P[:, 512:1024],
                        op=ALU.add), inc=cm.sv)
                    st["pg_free"] = cm.sv.n

    for s in range(NSLAB):
        kb.dma("sync", xs3, cfg["x_in"][:, :, s * TS:(s + 1) * TS]
               .rearrange("c p t -> p c t"), inc=cm.sq_ld)
        xld = cm.sq_ld.n
        kb.wait("vector", [(cm.sq_ld, xld), (cm.sp, getattr(cm, "_w3_used", 0))])
        vxn = cm.emit_ln()
        kb.wait("tensor", [(cm.sv, vxn)])
        proj(xb3, cfg["wq"], NC,
             bias[0] if bias else None, cfg.get("qscale"),
             (rope[0], rope[1]) if rope else None, s, xn3)
        kb.wait("sync", [(cm.sv, cm.sv.n), (cm.ss, cm.ss.n)])
        kb.dma("sync", cfg["qsc"][:, :, s * TS:(s + 1) * TS]
               .rearrange("c p t -> p c t"), xb3, inc=cm.sst)
        qsts = cm.sst.n
        proj(w23, cfg["wk"], nkv,
             bias[1] if bias else None, None,
             (rope[2], rope[3]) if rope else None, s, xn3)
        kb.wait("sync", [(cm.sv, cm.sv.n), (cm.ss, cm.ss.n)])
        kb.dma("sync", cfg["ksc"][:, :, s * TS:(s + 1) * TS]
               .rearrange("c p t -> p c t"),
               w23[:, 0:nkv, :], inc=cm.sst)
        xb_free_tgt = cm.sst.n
        proj(w23, cfg["wv"], nkv,
             bias[2] if bias else None, None, None, s, xn3)
        vvev = max(cm.ss.n, 1)
        vvev_v = cm.sv.n
        xb_free = xb_free_tgt
        for tt in range(4):
            c = 0
            while c < nkv:
                j0 = c
                j1 = min(c + 8, nkv)
                for c2 in range(j0, j1):
                    w = []
                    if c2 == j0:
                        w = [(cm.ss, vvev), (cm.sv, max(st["tp_free"], vvev_v))]
                    kb.op("tensor", lambda t, c2=c2, tt=tt, j0=j0: t.transpose(
                        cm.TP[:, (c2 - j0) * 128:(c2 - j0 + 1) * 128],
                        cm.W2[:, c2 * 512 + tt * 128: c2 * 512 + tt * 128 + 128],
                        cm.c_id[:]),
                        waits=w, inc=cm.sp if c2 == j1 - 1 else None)
                ptr = cm.sp.n
                kb.op("vector", lambda v, j0=j0, j1=j1: v.tensor_copy(
                    cm.XB[:, j0 * 128:j1 * 128],
                    cm.TP[:, 0:(j1 - j0) * 128]),
                    waits=[(cm.sp, ptr), (cm.sst, xb_free)], inc=cm.sv)
                st["tp_free"] = cm.sv.n
                c = j1
            vtt = cm.sv.n
            kb.dma("sync", cfg["vsc"][s * 4 + tt], cm.XB[:, 0:nkv * 128],
                   waits=[(cm.sv, vtt)], inc=cm.sst)
            xb_free = cm.sst.n
    kb.wait("sync", [(cm.sst, cm.sst.n)])

    for g in range(NSLAB):
        gch = (g + 1) if causal else NSLAB
        ntc = gch * 4
        kb.dma("sync", xb3, cfg["qsc"][:, :, g * TS:(g + 1) * TS]
               .rearrange("c p t -> p c t"), inc=cm.sq_ld)
        qld = cm.sq_ld.n
        kb.wait("tensor", [(cm.sq_ld, qld)])
        o_banks = [cm.PA[:], cm.PB[:], cm.PG[:], cm.SC[:, 0:512]]
        kbufs = [cm.W2[:, 0:2048], cm.W2[:, 2048:4096]]
        vbufs = [cm.W2[:, 4096:6144], cm.W2[:, 6144:8192]]
        last_kv = -1
        kv_ld = 0
        for hgrp in range(4):
            heads = [hgrp * 4 + i for i in range(4)]
            for hi, h in enumerate(heads):
                kv = h * nkv // NC
                kbuf = kbufs[kv % 2]
                if kv != last_kv:
                    kb.dma("sync", kbuf[:, 0:gch * 512],
                           cfg["ksc"][kv][:, 0:gch * 512],
                           waits=[(cm.sp, st["kbuf_used"][kv % 2])],
                           inc=cm.skp[kv % 2])
                    kv_ld = (cm.skp[kv % 2], cm.skp[kv % 2].n)
                    last_kv = kv
                ptq = cm.HG[:, hi * 8192:(hi + 1) * 8192].rearrange(
                    "p (a t) -> p a t", a=16)
                for qi in range(4):
                    sc = cm.SC[:, 0:gch * 512]
                    w0 = [kv_ld, (cm.ss, st["sc_free"])]
                    for kc in range(gch):
                        kb.op("tensor", lambda t, qi=qi, kc=kc, h=h, kbuf=kbuf: t.matmul(
                            cm.SC[:, kc * 512:(kc + 1) * 512],
                            xb3[:, h, qi * 128:qi * 128 + 128],
                            kbuf[:, kc * 512:(kc + 1) * 512],
                            start=True, stop=True),
                            waits=w0 if kc == 0 else [],
                            inc=cm.sp if kc == gch - 1 else None)
                    psc = cm.sp.n
                    st["kbuf_used"][kv % 2] = psc
                    if causal:
                        r = 128 * qi
                        kb.op("vector", lambda v, gch=gch, r=r: v.tensor_tensor(
                            cm.SC[:, (gch - 1) * 512:gch * 512],
                            cm.SC[:, (gch - 1) * 512:gch * 512],
                            cm.t_mask[:, 512 - r:1024 - r], op=ALU.add),
                            waits=[(cm.sp, psc)], inc=cm.sv)
                    kb.op("vector", lambda v, sc=sc: v.reduce_max(
                        cm.col_t0[:], sc, axis=AX.X, negate=True),
                        waits=[(cm.sp, psc)], inc=cm.sv)
                    vmx = cm.sv.n
                    kb.op("scalar", lambda s_, sc=sc, gch=gch: s_.activation(
                        cm.P[:, 0:gch * 512], sc, AF.Exp,
                        bias=cm.col_t0[:, 0:1], accum_out=cm.col_t1[:, 0:1]),
                        waits=[(cm.sv, vmx), (cm.sp, st["p_free"])], inc=cm.ss)
                    sexp = cm.ss.n
                    st["sc_free"] = sexp
                    kb.op("vector", lambda v: v.reciprocal(
                        cm.col_t2[:], cm.col_t1[:]),
                        waits=[(cm.ss, sexp)], inc=cm.sv)
                    kb.op("vector", lambda v, gch=gch: v.tensor_scalar(
                        cm.P[:, 0:gch * 512], cm.P[:, 0:gch * 512],
                        cm.col_t2[:, 0:1], None, op0=ALU.mult), inc=cm.sv)
                    vpn = cm.sv.n
                    tc = 0
                    while tc < ntc:
                        j0 = tc
                        j1 = min(tc + 8, ntc)
                        for t2 in range(j0, j1):
                            w = []
                            if t2 == j0:
                                w = [(cm.sv, vpn), (cm.sv, st["tp_free"])]
                            kb.op("tensor", lambda t, t2=t2, j0=j0: t.transpose(
                                cm.TP[:, (t2 - j0) * 128:(t2 - j0 + 1) * 128],
                                cm.P[:, t2 * 128:(t2 + 1) * 128], cm.c_id[:]),
                                waits=w, inc=cm.sp if t2 == j1 - 1 else None)
                        ptr = cm.sp.n
                        kb.op("vector", lambda v, j0=j0, j1=j1, qi=qi,
                              ptq=ptq: v.tensor_copy(
                            ptq[:, j0:j1, qi * 128:qi * 128 + 128],
                            cm.TP[:, 0:(j1 - j0) * 128].rearrange(
                                "p (a d) -> p a d", d=128)),
                            waits=[(cm.sp, ptr)], inc=cm.sv)
                        st["tp_free"] = cm.sv.n
                        tc = j1
                    st["p_free"] = cm.sp.n
            vpt_all = cm.sv.n
            for tc in range(ntc):
                vbuf = vbufs[tc % 2]
                kb.dma("sync", vbuf[:, 0:nkv * 128], cfg["vsc"][tc],
                       waits=[(cm.sp, st["vbuf_used"][tc % 2])],
                       inc=cm.svp[tc % 2])
                vld = (cm.svp[tc % 2], cm.svp[tc % 2].n)
                for hi, h in enumerate(heads):
                    kv = h * nkv // NC
                    ptq = cm.HG[:, hi * 8192:(hi + 1) * 8192].rearrange(
                        "p (a t) -> p a t", a=16)
                    w = []
                    if hi == 0:
                        w += [vld, (cm.sv, vpt_all)]
                    if tc == 0:
                        w += [(cm.ss, st["ob_free"][hi])]
                    kb.op("tensor", lambda t, tc=tc, hi=hi, kv=kv, ptq=ptq,
                          vbuf=vbuf: t.matmul(
                        o_banks[hi], vbuf[:, kv * 128:(kv + 1) * 128],
                        ptq[:, tc, :], start=(tc == 0), stop=(tc == ntc - 1)),
                        waits=w,
                        inc=cm.sp if hi == 3 else None)
                st["vbuf_used"][tc % 2] = cm.sp.n
            ppv = cm.sp.n
            for hi, h in enumerate(heads):
                kb.op("scalar", lambda s_, hi=hi, h=h: s_.activation(
                    xn3[:, h, :], o_banks[hi], AF.Copy),
                    waits=[(cm.sp, ppv)], inc=cm.ss)
                st["ob_free"][hi] = cm.ss.n
            st["sc_free"] = max(st["sc_free"], st["ob_free"][3])
        so_all = cm.ss.n
        if cfg.get("osc") is not None:
            kb.dma("sync", cfg["osc"][:, :, g * TS:(g + 1) * TS]
                   .rearrange("c p t -> p c t"), xn3,
                   waits=[(cm.ss, so_all)], inc=cm.sst)
        kb.dma("sync", xs3, cfg["x_in"][:, :, g * TS:(g + 1) * TS]
               .rearrange("c p t -> p c t"), inc=cm.sq_ld)
        xld = cm.sq_ld.n
        if cfg.get("wo") is not None:
            pp = [cm.PA[:], cm.PB[:]]
            pfree = [st["ob_free"][0], st["ob_free"][1]]
            for ocg in range(4):
                kb.wait("sync", [(cm.sp, cm._w12_used)])
                kb.dma("sync", cm.W1[:, 0:8192].rearrange(
                    "p (c n) -> p c n", c=NC),
                    cfg["wo"][:, :, ocg * 512:(ocg + 1) * 512]
                    .rearrange("c p n -> p c n"), inc=cm.sw)
                wld = cm.sw.n
                for oi in range(4):
                    oc = ocg * 4 + oi
                    po = pp[oc % 2]
                    w = ([(cm.sw, wld), (cm.ss, so_all)] if oi == 0 else [])
                    if pfree[oc % 2]:
                        w = w + [(cm.sv, pfree[oc % 2])]
                    for c in range(NC):
                        kb.op("tensor", lambda t, c=c, oi=oi, po=po: t.matmul(
                            po, cm.W1[:, c * 512 + oi * 128:
                                      c * 512 + oi * 128 + 128],
                            xn3[:, c, :], start=(c == 0), stop=(c == NC - 1)),
                            waits=w if c == 0 else [],
                            inc=cm.sp if c == NC - 1 else None)
                    pmm = cm.sp.n
                    cm._w12_used = pmm
                    w2 = [(cm.sp, pmm)]
                    if oc == 0:
                        w2.append((cm.sq_ld, xld))
                    if bias is not None and bias[3] is not None:
                        kb.op("vector", lambda v, oc=oc, po=po: v.scalar_tensor_tensor(
                            xs3[:, oc, :], po, bias[3][:, oc:oc + 1],
                            xs3[:, oc, :], op0=ALU.add, op1=ALU.add),
                            waits=w2, inc=cm.sv)
                    else:
                        kb.op("vector", lambda v, oc=oc, po=po: v.tensor_tensor(
                            xs3[:, oc, :], xs3[:, oc, :], po, op=ALU.add),
                            waits=w2, inc=cm.sv)
                    pfree[oc % 2] = cm.sv.n
            st["ob_free"] = [0, 0, 0, 0]
            st["po_evac"] = [None, None]
        else:
            kb.wait("vector", [(cm.sq_ld, xld)])
            kb.wait("tensor", [(cm.ss, so_all)])
            vxn2 = cm.emit_ln(dst=cm.W1)
            xnq = _view3(cm.W1)
            for oc in range(NC):
                kb.op("vector", lambda v, oc=oc: v.tensor_tensor(
                    xs3[:, oc, :], xs3[:, oc, :], xn3[:, oc, :], op=ALU.add),
                    waits=[(cm.ss, so_all), (cm.sv, vxn2)] if oc == 0 else [],
                    inc=cm.sv)
                kb.op("vector", lambda v, oc=oc: v.tensor_tensor(
                    xs3[:, oc, :], xs3[:, oc, :], xnq[:, oc, :], op=ALU.add),
                    inc=cm.sv)
            st["ob_free"] = [0, 0, 0, 0]
        vres = cm.sv.n
        kb.dma("sync", cfg["x_out"][:, :, g * TS:(g + 1) * TS]
               .rearrange("c p t -> p c t"), xs3,
               waits=[(cm.sv, vres)], inc=cm.sst)
    kb.wait("sync", [(cm.sst, cm.sst.n)])


def emit_prep_cast(kb, cm, dst, src):
    kb.dma("gpsimd", dst, src, inc=cm.spr)
    return cm.spr.n


def emit_prep_transpose(kb, cm, dstT, src_bf, K, N, wait_n=None, tilecols=None):
    kb.fence_sync(cm)
    stage = cm.HG[:, 0:N]
    allc = cm.spr.n
    for kc in range(K // 128):
        kb.dma("sync", stage, src_bf[0:N, kc * 128:(kc + 1) * 128],
               waits=[(cm.spr, allc)] if kc == 0 else [(cm.sst, cm.sst.n)],
               inc=cm.sq_ld, transpose=True)
        lded = cm.sq_ld.n
        kb.dma("sync", dstT[kc], stage, waits=[(cm.sq_ld, lded)], inc=cm.sst)
    return cm.sst.n


def emit_prep_down(kb, cm, dstT, src_bf, Dd, H, wait_n=None):
    kb.fence_sync(cm)
    stage = cm.HG[:, 0:Dd]
    allc = cm.spr.n
    HC = H // 128
    for hc in range(HC):
        kb.dma("sync", stage, src_bf[0:Dd, hc * 128:(hc + 1) * 128],
               waits=[(cm.spr, allc)] if hc == 0 else [(cm.sst, cm.sst.n)],
               inc=cm.sq_ld, transpose=True)
        lded = cm.sq_ld.n
        kb.dma("sync", dstT[:, :, hc, :].rearrange("dc p d -> p dc d"),
               stage.rearrange("p (dc d) -> p dc d", dc=NC),
               waits=[(cm.sq_ld, lded)], inc=cm.sst)
    return cm.sst.n


def build_l0_ffn():
    nc = bass.Bass("TRN2", target_bir_lowering=False)
    ctx = ExitStack()
    kb = KB(nc, ctx)
    H = 8192
    x_in = nc.dram_tensor("x_in", [NC, 128, S], F32, kind="ExternalInput").ap()
    wg = nc.dram_tensor("w_gate", [H, D], F32, kind="ExternalInput").ap()
    wu = nc.dram_tensor("w_up", [H, D], F32, kind="ExternalInput").ap()
    wd = nc.dram_tensor("w_down", [D, H], F32, kind="ExternalInput").ap()
    x_out = nc.dram_tensor("x_out", [NC, 128, S], F32, kind="ExternalOutput").ap()

    wg_bf = nc.dram_tensor("wg_bf", [H, D], BF16).ap()
    wu_bf = nc.dram_tensor("wu_bf", [H, D], BF16).ap()
    wd_bf = nc.dram_tensor("wd_bf", [D, H], BF16).ap()
    gateT = nc.dram_tensor("gateT", [NC, 128, H], BF16).ap()
    upT = nc.dram_tensor("upT", [NC, 128, H], BF16).ap()
    downT = nc.dram_tensor("downT", [NC, 128, H // 128, 128], BF16).ap()

    with ctx:
        cm = Common(kb)
        cm.emit_init()
        n1 = emit_prep_cast(kb, cm, wg_bf, wg)
        n2 = emit_prep_cast(kb, cm, wu_bf, wu)
        n3 = emit_prep_cast(kb, cm, wd_bf, wd)
        emit_prep_transpose(kb, cm, gateT, wg_bf, D, H, n1)
        emit_prep_transpose(kb, cm, upT, wu_bf, D, H, n2)
        emit_prep_down(kb, cm, downT, wd_bf, D, H, n3)
        prep_done = cm.sst.n
        kb.wait("sync", [(cm.sst, prep_done)])
        kb._g_used = 0
        kb._d_used = 0
        emit_ffn_glu(cm, kb, dict(
            H=H, gateT=gateT, upT=upT, downT=downT,
            x_in=x_in, x_out=x_out, act="silu"))
        kb.finalize()
    return nc


def emit_load_cols(cm, kb, vec_ap, ncols, dst):
    kb.fence_sync(cm)
    pgf = getattr(cm, "_pg_cols_free", 0)
    for base in range(0, ncols, 8):
        nn = min(8, ncols - base)
        kb.dma("sync", cm.rowP[0:1, 0:nn * 128],
               vec_ap[base * 128:(base + nn) * 128]
               .rearrange("(o n) -> o n", o=1),
               waits=[(cm.sv, pgf)], inc=cm.sq_ld)
        ldn = cm.sq_ld.n
        for j in range(nn):
            w = [(cm.sq_ld, ldn), (cm.ss, cm.ss.n)] if j == 0 else []
            if j == 0 and pgf:
                w.append((cm.sv, pgf))
            kb.op("tensor", lambda t, j=j: t.transpose(
                cm.PG[:, j:j + 1], cm.rowP[0:1, j * 128:(j + 1) * 128],
                cm.id32[0:1, 0:1]), waits=w,
                inc=cm.sp if j == nn - 1 else None)
        ptn = cm.sp.n
        kb.op("vector", lambda v, base=base, nn=nn: v.tensor_copy(
            dst[:, base:base + nn], cm.PG[:, 0:nn]),
            waits=[(cm.sp, ptn)], inc=cm.sv)
        pgf = cm.sv.n
    cm._pg_cols_free = pgf
    return cm.sv.n


def emit_dequant_mx(cm, kb, codes_ap, scales_ap, dst):
    kb.fence_sync(cm)
    sbc = cm.XS[:, 0:2048]
    cod = cm.XB[:, 0:2048]
    prev_st = getattr(cm, "_deq_st", 0)
    for kc in range(NC):
        kb.dma("gpsimd", cod, codes_ap[kc * 128:(kc + 1) * 128, :],
               waits=[(cm.sst, prev_st)], inc=cm.spr)
        kb.dma("sync", sbc,
               scales_ap[kc * 4:(kc + 1) * 4].rearrange(
                   "g (o n) -> g o n", o=1).broadcast_to([4, 32, 2048]),
               waits=[(cm.sst, prev_st)], inc=cm.sq_ld)
        ldc = cm.spr.n
        lds = cm.sq_ld.n
        kb.op("vector", lambda v: v.tensor_tensor(
            cod, cod, sbc, op=ALU.mult),
            waits=[(cm.spr, ldc), (cm.sq_ld, lds)], inc=cm.sv)
        vm = cm.sv.n
        kb.dma("sync", dst[kc], cod, waits=[(cm.sv, vm)], inc=cm.sst)
        prev_st = cm.sst.n
    cm._deq_st = prev_st
    kb.wait("sync", [(cm.sst, cm.sst.n)])


def emit_sign_prep(cm, kb, w_ap, sgn_bf, t_amax):
    kb.fence_sync(cm)
    prev_st = 0
    for oc in range(NC):
        kb.dma("sync", cm.XS[:, 0:2048], w_ap[oc * 128:(oc + 1) * 128, :],
               waits=[(cm.sst, prev_st)], inc=cm.sq_ld)
        ldn = cm.sq_ld.n
        wt = cm.XS[:, 0:2048]
        kb.op("scalar", lambda s: s.activation(
            cm.XB[:, 0:2048], wt, AF.Sign),
            waits=[(cm.sq_ld, ldn)], inc=cm.ss)
        sg = cm.ss.n
        kb.op("vector", lambda v, oc=oc: v.reduce_max(
            t_amax[:, oc:oc + 1], wt, axis=AX.X,
            apply_absolute_value=True),
            waits=[(cm.sq_ld, ldn)], inc=cm.sv)
        kb.dma("sync", sgn_bf[oc * 128:(oc + 1) * 128, :], cm.XB[:, 0:2048],
               waits=[(cm.ss, sg)], inc=cm.sst)
        prev_st = cm.sst.n
    va = cm.sv.n
    kb.op("vector", lambda v: v.reduce_max(
        cm.col_t0[:], t_amax[:, 0:NC], axis=AX.X), waits=[(cm.sv, va)],
        inc=cm.sv)
    vc = cm.sv.n
    kb.op("tensor", lambda t: t.transpose(
        cm.PG[0:1, 0:128], cm.col_t0[:], cm.id32[:]),
        waits=[(cm.sv, vc), (cm.sv, getattr(cm, "_pg_cols_free", 0))],
        inc=cm.sp)
    pt = cm.sp.n
    kb.op("vector", lambda v: v.reduce_max(
        cm.row_t0[0:1, 0:1], cm.PG[0:1, 0:128], axis=AX.X),
        waits=[(cm.sp, pt)], inc=cm.sv)
    vr = cm.sv.n
    kb.op("vector", lambda v: v.tensor_copy(
        cm.row_a[0:1, 0:1], cm.row_t0[0:1, 0:1]), inc=cm.sv)
    vrb = cm.sv.n
    kb.op("tensor", lambda t: t.matmul(
        cm.PG[:, 0:1], cm.ones_row[:], cm.row_a[0:1, 0:1],
        start=True, stop=True), waits=[(cm.sv, vrb)], inc=cm.sp)
    pb = cm.sp.n
    kb.op("vector", lambda v: v.tensor_copy(cm.col_t0[:], cm.PG[:, 0:1]),
          waits=[(cm.sp, pb)], inc=cm.sv)
    cm._pg_cols_free = cm.sv.n
    return cm.sv.n


def emit_decode_nvfp4(cm, kb, packed_ap, scales_ap, dec_ap, nrows, K):
    kb.fence_sync(cm)
    NB = K // 2
    i32 = cm.W3[:].bitcast(I32)
    B = i32[:, 0:1024]
    Mlo = i32[:, 1024:2048]
    Mhi = i32[:, 2048:3072]
    Sx = i32[:, 3072:4096]
    T1 = cm.XS[:, 0:1024]
    SGN = cm.XS[:, 1024:2048]
    VLO = cm.XS[:, 2048:3072]
    VHI = cm.XS[:, 3072:4096]
    SCT = cm.XS[:, 4096:4160]
    out = cm.HG[:, 0:2048]
    prev_st = getattr(cm, "_dec_st", 0)
    for r in range(nrows // 128):
        for cb in range(NB // 1024):
            kb.dma("sync", B, packed_ap[r * 128:(r + 1) * 128,
                                        cb * 1024:(cb + 1) * 1024],
                   waits=[(cm.sst, prev_st)], inc=cm.sq_ld)
            kb.dma("sync", SCT, scales_ap[r * 128:(r + 1) * 128,
                                          cb * 64:(cb + 1) * 64],
                   inc=cm.sq_ld)
            ldn = cm.sq_ld.n
            v = cm.sv
            kb.op("vector", lambda x: x.tensor_scalar(
                Mlo, B, 7, None, op0=ALU.bitwise_and),
                waits=[(cm.sq_ld, ldn)])
            kb.op("vector", lambda x: x.tensor_scalar(
                Mhi, B, 4, 7, op0=ALU.logical_shift_right,
                op1=ALU.bitwise_and))
            kb.op("vector", lambda x: x.tensor_scalar(
                Sx, B, 8, None, op0=ALU.bitwise_and))
            kb.op("vector", lambda x: x.tensor_scalar(
                SGN, Sx, -0.25, 1.0, op0=ALU.mult, op1=ALU.add))
            kb.op("vector", lambda x: x.tensor_scalar(
                T1, Mlo, 2.0, -8.0, op0=ALU.mult, op1=ALU.add))
            kb.op("vector", lambda x: x.scalar_tensor_tensor(
                T1, Mlo, -2.0, T1, op0=ALU.add, op1=ALU.max))
            kb.op("vector", lambda x: x.scalar_tensor_tensor(
                T1, Mlo, 0.5, T1, op0=ALU.mult, op1=ALU.max))
            kb.op("vector", lambda x: x.tensor_tensor(
                VLO, T1, SGN, op=ALU.mult))
            kb.op("vector", lambda x: x.tensor_scalar(
                Sx, B, 4, 8, op0=ALU.logical_shift_right,
                op1=ALU.bitwise_and))
            kb.op("vector", lambda x: x.tensor_scalar(
                SGN, Sx, -0.25, 1.0, op0=ALU.mult, op1=ALU.add))
            kb.op("vector", lambda x: x.tensor_scalar(
                T1, Mhi, 2.0, -8.0, op0=ALU.mult, op1=ALU.add))
            kb.op("vector", lambda x: x.scalar_tensor_tensor(
                T1, Mhi, -2.0, T1, op0=ALU.add, op1=ALU.max))
            kb.op("vector", lambda x: x.scalar_tensor_tensor(
                T1, Mhi, 0.5, T1, op0=ALU.mult, op1=ALU.max))
            kb.op("vector", lambda x: x.tensor_tensor(
                VHI, T1, SGN, op=ALU.mult))
            scv = SCT.rearrange("p (b o) -> p b o", o=1).broadcast_to(
                [128, 64, 16])
            o2 = out.rearrange("p (b2 two) -> p b2 two", two=2)
            kb.op("vector", lambda x: x.tensor_tensor(
                o2[:, :, 0:1].rearrange("p a o -> p (a o)")
                .rearrange("p (b o) -> p b o", b=64),
                VLO.rearrange("p (b o) -> p b o", b=64), scv, op=ALU.mult))
            kb.op("vector", lambda x: x.tensor_tensor(
                o2[:, :, 1:2].rearrange("p a o -> p (a o)")
                .rearrange("p (b o) -> p b o", b=64),
                VHI.rearrange("p (b o) -> p b o", b=64), scv, op=ALU.mult),
                inc=v)
            vd = v.n
            kb.dma("sync", dec_ap[r * 128:(r + 1) * 128,
                                  cb * 2048:(cb + 1) * 2048], out,
                   waits=[(cm.sv, vd)], inc=cm.sst)
            prev_st = cm.sst.n
    cm._dec_st = prev_st
    kb.wait("sync", [(cm.sst, cm.sst.n)])


def emit_ffn_l1(cm, kb, cfg):
    xs3 = _view3(cm.XS)
    xn3 = _view3(cm.XN)
    xb3 = _view3(cm.XB)
    t0 = cm.P[:].bitcast(F32)[:, 0:512]
    kb.fence_sync(cm)
    for s in range(NSLAB):
        kb.dma("sync", xs3, cfg["x_in"][:, :, s * TS:(s + 1) * TS]
               .rearrange("c p t -> p c t"), inc=cm.sq_ld)
        xld = cm.sq_ld.n
        kb.wait("vector", [(cm.sq_ld, xld), (cm.sp, cm._w3_used)])
        cm.emit_ln()
        vxn2 = cm.emit_ln(src=xn3, src_is_bf16=True,
                          affine_cols=(cfg["lnw_cols"], cfg["lnb_cols"]),
                          dst=cm.XB)
        kb.wait("tensor", [(cm.sv, vxn2)])
        pp = [cm.PA[:], cm.PB[:]]
        pfree = [0, 0]
        for ocg in range(4):
            kb.wait("sync", [(cm.sp, cm._w12_used)])
            kb.dma("sync", cm.W1[:, 0:8192].rearrange(
                "p (c n) -> p c n", c=NC),
                cfg["wT"][:, :, ocg * 512:(ocg + 1) * 512]
                .rearrange("c p n -> p c n"), inc=cm.sw)
            wld = cm.sw.n
            for oi in range(4):
                oc = ocg * 4 + oi
                po = pp[oc % 2]
                w = [(cm.sw, wld)] if oi == 0 else []
                if pfree[oc % 2]:
                    w = w + [(cm.sv, pfree[oc % 2])]
                for c in range(NC):
                    kb.op("tensor", lambda t, c=c, oi=oi, po=po: t.matmul(
                        po, cm.W1[:, c * 512 + oi * 128:
                                  c * 512 + oi * 128 + 128],
                        xb3[:, c, :], start=(c == 0), stop=(c == NC - 1)),
                        waits=w if c == 0 else [],
                        inc=cm.sp if c == NC - 1 else None)
                pmm = cm.sp.n
                cm._w12_used = pmm
                kb.op("vector", lambda v, oc=oc, po=po: v.tensor_scalar(
                    t0, po, cfg["sg_cols"][:, oc:oc + 1],
                    cfg["hb_cols"][:, oc:oc + 1],
                    op0=ALU.mult, op1=ALU.add),
                    waits=[(cm.sp, pmm)], inc=cm.sv)
                kb.op("vector", lambda v, oc=oc: v.tensor_tensor(
                    xs3[:, oc, :], xs3[:, oc, :], t0, op=ALU.add),
                    inc=cm.sv)
                pfree[oc % 2] = cm.sv.n
        vres = cm.sv.n
        kb.dma("sync", cfg["x_out"][:, :, s * TS:(s + 1) * TS]
               .rearrange("c p t -> p c t"), xs3,
               waits=[(cm.sv, vres)], inc=cm.sst)
    kb.wait("sync", [(cm.sst, cm.sst.n)])


def build_l0_attn(debug=False):
    nc = bass.Bass("TRN2", target_bir_lowering=False)
    ctx = ExitStack()
    kb = KB(nc, ctx)
    x_in = nc.dram_tensor("x_in", [NC, 128, S], F32, kind="ExternalInput").ap()
    Wq = nc.dram_tensor("Wq", [D, D], F32, kind="ExternalInput").ap()
    Wk = nc.dram_tensor("Wk", [D, 512], F32, kind="ExternalInput").ap()
    Wv = nc.dram_tensor("Wv", [D, 512], F32, kind="ExternalInput").ap()
    Wo = nc.dram_tensor("Wo", [D, D], F32, kind="ExternalInput").ap()
    x_out = nc.dram_tensor("x_out", [NC, 128, S], F32, kind="ExternalOutput").ap()
    tabs = {}
    for nm in ("cq", "sq", "ck", "sk"):
        tabs[nm] = nc.dram_tensor("c_" + nm, [128, S], BF16,
                                  kind="ExternalInput").ap()

    wq_bf = nc.dram_tensor("wq_bf", [NC, 128, D], BF16).ap()
    wk_bf = nc.dram_tensor("wk_bf", [NC, 128, 512], BF16).ap()
    wv_bf = nc.dram_tensor("wv_bf", [NC, 128, 512], BF16).ap()
    wo_bf = nc.dram_tensor("wo_bf", [NC, 128, D], BF16).ap()
    dbg = "ExternalOutput" if debug else "Internal"
    osc = nc.dram_tensor("osc", [NC, 128, S], BF16, kind=dbg).ap() if debug else None
    qsc = nc.dram_tensor("qsc", [NC, 128, S], BF16, kind=dbg).ap()
    ksc = nc.dram_tensor("ksc", [4, 128, S], BF16, kind=dbg).ap()
    vsc = nc.dram_tensor("vsc", [NC, 128, 512], BF16, kind=dbg).ap()

    with ctx:
        cm = Common(kb, need=("attn",))
        t_tabs = {nm: kb.sb("t_" + nm, [128, S], BF16) for nm in tabs}
        cm.emit_init()
        for nm in tabs:
            kb.dma("sync", t_tabs[nm][:], tabs[nm], inc=cm.sq_ld)
        emit_prep_cast(kb, cm, wq_bf.rearrange("c p n -> (c p) n"), Wq)
        emit_prep_cast(kb, cm, wk_bf.rearrange("c p n -> (c p) n"), Wk)
        emit_prep_cast(kb, cm, wv_bf.rearrange("c p n -> (c p) n"), Wv)
        pn = emit_prep_cast(kb, cm, wo_bf.rearrange("c p n -> (c p) n"), Wo)
        kb.wait("sync", [(cm.spr, pn), (cm.sq_ld, cm.sq_ld.n)])
        kb.wait("tensor", [(cm.spr, pn)])
        emit_attn(cm, kb, dict(
            kind="l0", nkv=4, causal=True,
            rope=(t_tabs["cq"], t_tabs["sq"], t_tabs["ck"], t_tabs["sk"]),
            wq=wq_bf, wk=wk_bf, wv=wv_bf, wo=wo_bf, osc=osc,
            qsc=qsc, ksc=ksc, vsc=vsc, x_in=x_in, x_out=x_out))
        kb.finalize()
    return nc


def _mkx(nc):
    x_in = nc.dram_tensor("x_in", [NC, 128, S], F32, kind="ExternalInput").ap()
    x_out = nc.dram_tensor("x_out", [NC, 128, S], F32,
                           kind="ExternalOutput").ap()
    return x_in, x_out


def build_l1_ffn(debug=False):
    nc = bass.Bass("TRN2", target_bir_lowering=False)
    ctx = ExitStack()
    kb = KB(nc, ctx)
    x_in, x_out = _mkx(nc)
    dbg = "ExternalOutput" if debug else "Internal"
    W = nc.dram_tensor("L1_ffn_weight", [D, D], F32, kind="ExternalInput").ap()
    gam = nc.dram_tensor("L1_ffn_gamma", [D], F32, kind="ExternalInput").ap()
    bet = nc.dram_tensor("L1_ffn_beta", [D], F32, kind="ExternalInput").ap()
    lnw = nc.dram_tensor("L1_ffn_ln_weight", [D], F32, kind="ExternalInput").ap()
    lnb = nc.dram_tensor("L1_ffn_ln_bias", [D], F32, kind="ExternalInput").ap()
    sgn_bf = nc.dram_tensor("sgn_bf", [D, D], BF16, kind=dbg).ap()
    wT = nc.dram_tensor("wT", [NC, 128, D], BF16, kind=dbg).ap()
    d_cols = (nc.dram_tensor("d_cols", [128, 4 * NC], F32,
                             kind="ExternalOutput").ap() if debug else None)
    with ctx:
        cm = Common(kb)
        t_amax = kb.sb("t_amax", [128, NC], F32)
        t_g = kb.sb("t_g", [128, NC], F32)
        t_b = kb.sb("t_b", [128, NC], F32)
        t_lnw = kb.sb("t_lnw", [128, NC], F32)
        t_lnb = kb.sb("t_lnb", [128, NC], F32)
        t_sg = kb.sb("t_sg", [128, NC], F32)
        t_hb = kb.sb("t_hb", [128, NC], F32)
        cm.emit_init()
        emit_load_cols(cm, kb, gam, NC, t_g)
        emit_load_cols(cm, kb, bet, NC, t_b)
        emit_load_cols(cm, kb, lnw, NC, t_lnw)
        vcols = emit_load_cols(cm, kb, lnb, NC, t_lnb)
        vsc = emit_sign_prep(cm, kb, W, sgn_bf, t_amax)
        kb.op("vector", lambda v: v.tensor_scalar(
            t_sg[:], t_g[:], cm.col_t0[:, 0:1], 0.5, op0=ALU.mult,
            op1=ALU.mult), waits=[(cm.sv, max(vcols, vsc))], inc=cm.sv)
        kb.op("vector", lambda v: v.tensor_scalar(
            t_hb[:], t_b[:], 0.5, None, op0=ALU.mult), inc=cm.sv)
        if debug:
            vv = cm.sv.n
            xf = cm.XB[:].bitcast(F32)
            kb.op("vector", lambda v: v.tensor_copy(xf[:, 0:16], t_sg[:]),
                  waits=[(cm.sv, vv)], inc=cm.sv)
            kb.op("vector", lambda v: v.tensor_copy(xf[:, 16:32], t_hb[:]),
                  inc=cm.sv)
            kb.op("vector", lambda v: v.tensor_copy(xf[:, 32:48], t_lnw[:]),
                  inc=cm.sv)
            kb.op("vector", lambda v: v.tensor_copy(xf[:, 48:64], t_lnb[:]),
                  inc=cm.sv)
            vv2 = cm.sv.n
            kb.dma("sync", d_cols, xf[:, 0:64],
                   waits=[(cm.sv, vv2)], inc=cm.sst)
        stq = emit_prep_transpose(kb, cm, wT, sgn_bf, D, D, 0)
        kb.wait("sync", [(cm.sst, stq)])
        kb.wait("tensor", [(cm.sst, stq), (cm.sv, cm.sv.n)])
        emit_ffn_l1(cm, kb, dict(
            wT=wT, sg_cols=t_sg, hb_cols=t_hb, lnw_cols=t_lnw,
            lnb_cols=t_lnb, x_in=x_in, x_out=x_out))
        kb.finalize()
    return nc


def build_l1_attn():
    nc = bass.Bass("TRN2", target_bir_lowering=False)
    ctx = ExitStack()
    kb = KB(nc, ctx)
    x_in, x_out = _mkx(nc)
    srcs = {}
    for nm in ("wq", "wk", "wv", "wo"):
        srcs[nm + "_codes"] = nc.dram_tensor(
            f"L1_attn_{nm}_codes", [D, D], F32, kind="ExternalInput").ap()
        srcs[nm + "_scales"] = nc.dram_tensor(
            f"L1_attn_{nm}_scales", [64, D], F32, kind="ExternalInput").ap()
    for nm in ("bq", "bk", "bv", "bo"):
        srcs[nm] = nc.dram_tensor(f"L1_attn_{nm}", [D], F32,
                                  kind="ExternalInput").ap()
    wE = {nm: nc.dram_tensor(f"wE_{nm}", [NC, 128, D], BF16).ap()
          for nm in ("wq", "wk", "wv", "wo")}
    qsc = nc.dram_tensor("qsc", [NC, 128, S], BF16).ap()
    ksc = nc.dram_tensor("ksc", [NC, 128, S], BF16).ap()
    vsc = nc.dram_tensor("vsc", [NC, 128, NC * 128], BF16).ap()
    with ctx:
        cm = Common(kb, need=("attn",))
        t_bias = {nm: kb.sb("t_" + nm, [128, NC], F32)
                  for nm in ("bq", "bk", "bv", "bo")}
        t_qs = kb.sb("t_qs", [128, 1], F32)
        cm.emit_init()
        kb.op("vector", lambda v: v.memset(t_qs[:], 1.0 / np.sqrt(HD)),
              inc=cm.sv)
        for nm in ("bq", "bk", "bv", "bo"):
            emit_load_cols(cm, kb, srcs[nm], NC, t_bias[nm])
        vb = cm.sv.n
        kb.op("vector", lambda v: v.tensor_scalar(
            t_bias["bq"][:], t_bias["bq"][:], 1.0 / np.sqrt(HD), None,
            op0=ALU.mult), waits=[(cm.sv, vb)], inc=cm.sv)
        for nm in ("wq", "wk", "wv", "wo"):
            emit_dequant_mx(cm, kb, srcs[nm + "_codes"],
                            srcs[nm + "_scales"], wE[nm])
        kb.wait("tensor", [(cm.sst, cm.sst.n), (cm.sv, cm.sv.n)])
        emit_attn(cm, kb, dict(
            kind="l1", nkv=NC, causal=False, rope=None,
            wq=wE["wq"], wk=wE["wk"], wv=wE["wv"], wo=wE["wo"],
            bias_cols=(t_bias["bq"], t_bias["bk"], t_bias["bv"],
                       t_bias["bo"]),
            qscale=t_qs, qsc=qsc, ksc=ksc, vsc=vsc,
            x_in=x_in, x_out=x_out))
        kb.finalize()
    return nc


def build_l2_ffn():
    nc = bass.Bass("TRN2", target_bir_lowering=False)
    ctx = ExitStack()
    kb = KB(nc, ctx)
    x_in, x_out = _mkx(nc)
    H = 5632
    HC = H // 128
    gw = nc.dram_tensor("L2_ffn_gate_w", [H, D], F32, kind="ExternalInput").ap()
    uw = nc.dram_tensor("L2_ffn_up_w", [H, D], F32, kind="ExternalInput").ap()
    dw = nc.dram_tensor("L2_ffn_down_w", [D, H], F32, kind="ExternalInput").ap()
    gs = nc.dram_tensor("L2_ffn_gate_s", [H], F32, kind="ExternalInput").ap()
    us = nc.dram_tensor("L2_ffn_up_s", [H], F32, kind="ExternalInput").ap()
    ds = nc.dram_tensor("L2_ffn_down_s", [D], F32, kind="ExternalInput").ap()
    g_bf = nc.dram_tensor("g_bf", [H, D], BF16).ap()
    u_bf = nc.dram_tensor("u_bf", [H, D], BF16).ap()
    d_bf = nc.dram_tensor("d_bf", [D, H], BF16).ap()
    gateT = nc.dram_tensor("gateT", [NC, 128, H], BF16).ap()
    upT = nc.dram_tensor("upT", [NC, 128, H], BF16).ap()
    downT = nc.dram_tensor("downT", [NC, 128, HC, 128], BF16).ap()
    with ctx:
        cm = Common(kb)
        t_gs = kb.sb("t_gs", [128, HC], F32)
        t_us = kb.sb("t_us", [128, HC], F32)
        t_ds = kb.sb("t_ds", [128, NC], F32)
        cm.emit_init()
        emit_load_cols(cm, kb, gs, HC, t_gs)
        emit_load_cols(cm, kb, us, HC, t_us)
        emit_load_cols(cm, kb, ds, NC, t_ds)
        n1 = emit_prep_cast(kb, cm, g_bf, gw)
        n2 = emit_prep_cast(kb, cm, u_bf, uw)
        n3 = emit_prep_cast(kb, cm, d_bf, dw)
        emit_prep_transpose(kb, cm, gateT, g_bf, D, H, n1)
        emit_prep_transpose(kb, cm, upT, u_bf, D, H, n2)
        emit_prep_down(kb, cm, downT, d_bf, D, H, n3)
        kb.wait("sync", [(cm.sst, cm.sst.n)])
        kb.wait("vector", [(cm.sst, cm.sst.n)])
        emit_ffn_glu(cm, kb, dict(
            H=H, gateT=gateT, upT=upT, downT=downT, act="silu",
            gate_s_cols=t_gs, up_s_cols=t_us, down_s_cols=t_ds,
            x_in=x_in, x_out=x_out))
        kb.finalize()
    return nc


def build_l2_attn():
    nc = bass.Bass("TRN2", target_bir_lowering=False)
    ctx = ExitStack()
    kb = KB(nc, ctx)
    x_in, x_out = _mkx(nc)
    src = {nm: nc.dram_tensor(f"L2_attn_{nm}", [D, D], F32,
                              kind="ExternalInput").ap()
           for nm in ("Wq", "Wk", "Wv", "Wo")}
    tabs = {nm: nc.dram_tensor("c_" + nm, [128, S], BF16,
                               kind="ExternalInput").ap()
            for nm in ("cq", "sq", "ck", "sk")}
    w_bf = {nm: nc.dram_tensor(f"{nm}_bf", [D, D], BF16).ap()
            for nm in ("Wq", "Wk", "Wv", "Wo")}
    wT = {nm: nc.dram_tensor(f"{nm}_T", [NC, 128, D], BF16).ap()
          for nm in ("Wq", "Wk", "Wv", "Wo")}
    qsc = nc.dram_tensor("qsc", [NC, 128, S], BF16).ap()
    ksc = nc.dram_tensor("ksc", [NC, 128, S], BF16).ap()
    vsc = nc.dram_tensor("vsc", [NC, 128, NC * 128], BF16).ap()
    with ctx:
        cm = Common(kb, need=("attn",))
        t_tabs = {nm: kb.sb("t_" + nm, [128, S], BF16) for nm in tabs}
        cm.emit_init()
        for nm in tabs:
            kb.dma("sync", t_tabs[nm][:], tabs[nm], inc=cm.sq_ld)
        for nm in ("Wq", "Wk", "Wv", "Wo"):
            n = emit_prep_cast(kb, cm, w_bf[nm], src[nm])
            emit_prep_transpose(kb, cm, wT[nm], w_bf[nm], D, D, n)
        kb.wait("sync", [(cm.sst, cm.sst.n), (cm.sq_ld, cm.sq_ld.n)])
        kb.wait("tensor", [(cm.sst, cm.sst.n)])
        emit_attn(cm, kb, dict(
            kind="l2", nkv=NC, causal=True,
            rope=(t_tabs["cq"], t_tabs["sq"], t_tabs["ck"], t_tabs["sk"]),
            wq=wT["Wq"], wk=wT["Wk"], wv=wT["Wv"], wo=wT["Wo"],
            qsc=qsc, ksc=ksc, vsc=vsc, x_in=x_in, x_out=x_out))
        kb.finalize()
    return nc


def build_l3_attn():
    nc = bass.Bass("TRN2", target_bir_lowering=False)
    ctx = ExitStack()
    kb = KB(nc, ctx)
    x_in, x_out = _mkx(nc)
    src = {nm: nc.dram_tensor(f"L3_attn_{nm}", [D, D], F32,
                              kind="ExternalInput").ap()
           for nm in ("Wq", "Wk", "Wv")}
    bsrc = {nm: nc.dram_tensor(f"L3_attn_{nm}", [D], F32,
                               kind="ExternalInput").ap()
            for nm in ("bq", "bk", "bv")}
    w_bf = {nm: nc.dram_tensor(f"{nm}_bf", [NC, 128, D], BF16).ap()
            for nm in ("Wq", "Wk", "Wv")}
    qsc = nc.dram_tensor("qsc", [NC, 128, S], BF16).ap()
    ksc = nc.dram_tensor("ksc", [NC, 128, S], BF16).ap()
    vsc = nc.dram_tensor("vsc", [NC, 128, NC * 128], BF16).ap()
    with ctx:
        cm = Common(kb, need=("attn",))
        t_bias = {nm: kb.sb("t_" + nm, [128, NC], F32)
                  for nm in ("bq", "bk", "bv")}
        t_qs = kb.sb("t_qs", [128, 1], F32)
        cm.emit_init()
        kb.op("vector", lambda v: v.memset(t_qs[:], 1.0 / np.sqrt(HD)),
              inc=cm.sv)
        for nm in ("bq", "bk", "bv"):
            emit_load_cols(cm, kb, bsrc[nm], NC, t_bias[nm])
        vb = cm.sv.n
        kb.op("vector", lambda v: v.tensor_scalar(
            t_bias["bq"][:], t_bias["bq"][:], 1.0 / np.sqrt(HD), None,
            op0=ALU.mult), waits=[(cm.sv, vb)], inc=cm.sv)
        for nm in ("Wq", "Wk", "Wv"):
            emit_prep_cast(kb, cm, w_bf[nm].rearrange("c p n -> (c p) n"),
                           src[nm])
        kb.wait("sync", [(cm.spr, cm.spr.n)])
        kb.wait("tensor", [(cm.spr, cm.spr.n), (cm.sv, cm.sv.n)])
        emit_attn(cm, kb, dict(
            kind="l3", nkv=NC, causal=False, rope=None,
            wq=w_bf["Wq"], wk=w_bf["Wk"], wv=w_bf["Wv"], wo=None,
            bias_cols=(t_bias["bq"], t_bias["bk"], t_bias["bv"], None),
            qscale=t_qs, qsc=qsc, ksc=ksc, vsc=vsc,
            x_in=x_in, x_out=x_out))
        kb.finalize()
    return nc


def build_l3_ffn():
    nc = bass.Bass("TRN2", target_bir_lowering=False)
    ctx = ExitStack()
    kb = KB(nc, ctx)
    x_in, x_out = _mkx(nc)
    H = 16384
    HC = H // 128
    gp = nc.dram_tensor("L3_ffn_gate_W_packed", [H, D // 2], I32,
                        kind="ExternalInput").ap()
    gsc = nc.dram_tensor("L3_ffn_gate_scales", [H, D // 32], F32,
                         kind="ExternalInput").ap()
    up = nc.dram_tensor("L3_ffn_up_W_packed", [H, D // 2], I32,
                        kind="ExternalInput").ap()
    usc = nc.dram_tensor("L3_ffn_up_scales", [H, D // 32], F32,
                         kind="ExternalInput").ap()
    dp = nc.dram_tensor("L3_ffn_down_W_packed", [D, H // 2], I32,
                        kind="ExternalInput").ap()
    dsc = nc.dram_tensor("L3_ffn_down_scales", [D, H // 32], F32,
                         kind="ExternalInput").ap()
    g_dec = nc.dram_tensor("g_dec", [H, D], BF16).ap()
    u_dec = nc.dram_tensor("u_dec", [H, D], BF16).ap()
    d_dec = nc.dram_tensor("d_dec", [D, H], BF16).ap()
    gateT = nc.dram_tensor("gateT", [NC, 128, H], BF16).ap()
    upT = nc.dram_tensor("upT", [NC, 128, H], BF16).ap()
    downT = nc.dram_tensor("downT", [NC, 128, HC, 128], BF16).ap()
    with ctx:
        cm = Common(kb)
        cm.emit_init()
        emit_decode_nvfp4(cm, kb, gp, gsc, g_dec, H, D)
        emit_decode_nvfp4(cm, kb, up, usc, u_dec, H, D)
        emit_decode_nvfp4(cm, kb, dp, dsc, d_dec, D, H)
        emit_prep_transpose(kb, cm, gateT, g_dec, D, H, 0)
        emit_prep_transpose(kb, cm, upT, u_dec, D, H, 0)
        emit_prep_down(kb, cm, downT, d_dec, D, H, 0)
        kb.wait("sync", [(cm.sst, cm.sst.n)])
        kb.wait("vector", [(cm.sst, cm.sst.n)])
        emit_ffn_glu(cm, kb, dict(
            H=H, gateT=gateT, upT=upT, downT=downT, act="gelu_tanh",
            outacc=True, x_in=x_in, x_out=x_out))
        kb.finalize()
    return nc


def _shard_x(x):
    out = []
    for c in range(x.shape[0]):
        xt = np.ascontiguousarray(x[c].T.astype(np.float32))
        out.append(xt.reshape(NC, 128, S))
    return out


def _unshard_x(parts):
    return np.stack([p.reshape(D, S).T for p in parts], axis=0)


_PROGS = {}


def _prog(name):
    if name not in _PROGS:
        _PROGS[name] = {
            "l0f": build_l0_ffn, "l0a": build_l0_attn,
            "l1f": build_l1_ffn, "l1a": build_l1_attn,
            "l2f": build_l2_ffn, "l2a": build_l2_attn,
            "l3f": build_l3_ffn, "l3a": build_l3_attn,
        }[name]()
    return _PROGS[name]


def kernel(**inputs):
    NCORES = 8
    consts = get_host_consts()
    xs = _shard_x(inputs["L0_attn_x"])

    def wmap(name):
        c = dict(c_identity=consts["identity"])
        if name == "l0f":
            c.update(w_gate=inputs["L0_ffn_w_gate"], w_up=inputs["L0_ffn_w_up"],
                     w_down=inputs["L0_ffn_w_down"])
        elif name == "l0a":
            c.update(Wq=inputs["L0_attn_Wq"], Wk=inputs["L0_attn_Wk"],
                     Wv=inputs["L0_attn_Wv"], Wo=inputs["L0_attn_Wo"],
                     c_rot=consts["rot"], c_mask=consts["mask"],
                     c_cq=consts["l0_cq"], c_sq=consts["l0_sq"],
                     c_ck=consts["l0_ck"], c_sk=consts["l0_sk"])
        elif name == "l1f":
            c.update({k: inputs[k] for k in (
                "L1_ffn_weight", "L1_ffn_gamma", "L1_ffn_beta",
                "L1_ffn_ln_weight", "L1_ffn_ln_bias")})
        elif name == "l1a":
            c.update({k: inputs[k] for k in inputs if k.startswith("L1_attn_")})
            c.update(c_rot=consts["rot"], c_mask=consts["mask"])
        elif name == "l2f":
            c.update({k: inputs[k] for k in inputs if k.startswith("L2_ffn_")})
        elif name == "l2a":
            c.update({k: inputs[k] for k in inputs if k.startswith("L2_attn_")})
            c.update(c_rot=consts["rot"], c_mask=consts["mask"],
                     c_cq=consts["l2_cq"], c_sq=consts["l2_sq"],
                     c_ck=consts["l2_ck"], c_sk=consts["l2_sk"])
        elif name == "l3f":
            c.update({k: inputs[k] for k in inputs if k.startswith("L3_ffn_")})
        elif name == "l3a":
            c.update({k: inputs[k] for k in inputs if k.startswith("L3_attn_")})
            c.update(c_rot=consts["rot"], c_mask=consts["mask"])
        return c

    steps = ["l0f", "l0a", "l0f", "l1f", "l1a", "l1f",
             "l2f", "l2a", "l2f", "l3f", "l3a", "l3f"]
    for name in steps:
        nc = _prog(name)
        wm = wmap(name)
        in_maps = [dict(wm, x_in=xs[c]) for c in range(NCORES)]
        res = run_bass_kernel_spmd(nc, in_maps, core_ids=list(range(NCORES)))
        xs = [np.asarray(res.results[c]["x_out"]) for c in range(NCORES)]
    return _unshard_x(xs).astype(np.float32)
